# version 01
# baseline (speedup 1.0000x reference)
import numpy as np
import ml_dtypes
import concourse.bass as bass
import concourse.mybir as mybir
from concourse.bass import IndirectOffsetOnAxis
from concourse.bass_utils import run_bass_kernel_spmd
from concourse.alu_op_type import AluOpType as aop

F32 = mybir.dt.float32
BF16 = mybir.dt.bfloat16
U32 = mybir.dt.uint32
AF = mybir.ActivationFunctionType

ENGINES = ("tensor", "vector", "scalar", "sync", "gpsimd")
NOWN, NEXT, NOTH = 1024, 1280, 768
D = 4096
RSCALE = float(1.0 / np.sqrt(np.float32(128.0)))
QBS = [(0, 512), (512, 512), (1024, 256)]


class PsumPool:
    def __init__(self, tensors):
        self.t = tensors
        self.guard = [None] * len(tensors)

    def acq(self, b, i):
        g = self.guard[i]
        if g is not None:
            b.wt("tensor", g[0], g[1])
        return self.t[i]

    def rel(self, i, sem, tgt):
        self.guard[i] = (sem, tgt)


class Bld:
    def __init__(self, nc):
        self.nc = nc
        self.cnt = {}
        self.last = {}
        self.sems = {}
        self.q = {e: [] for e in ENGINES}

    def bump(self, sem, n):
        c = self.cnt.get(sem, 0) + n
        self.cnt[sem] = c
        return c

    def ins(self, eng, fn, sem=None, inc=1):
        tgt = None
        if sem is not None:
            tgt = self.bump(sem, inc)
            self.last[eng] = (sem, tgt)

        def f(e, fn=fn, sem=sem, inc=inc):
            r = fn(e)
            if sem is not None:
                r.then_inc(self.sems[sem], inc)
        self.q[eng].append(f)
        return tgt

    def wt(self, eng, sem, tgt):
        if tgt is None or tgt <= 0:
            return
        self.q[eng].append(lambda e, s=sem, t=tgt: e.wait_ge(self.sems[s], t))

    def dma(self, eng, sem, outap, inap, **kw):
        return self.ins(eng, lambda e, o=outap, i=inap, kw=kw:
                        e.dma_start(out=o, in_=i, **kw), sem, 16)

    def dmat(self, outap, inap):
        return self.ins("sync", lambda e, o=outap, i=inap:
                        e.dma_start_transpose(o, i), "st", 16)

    def barrier(self):
        waits = {}
        for e2 in ENGINES:
            li = self.last.get(e2)
            if li:
                waits[li[0]] = max(waits.get(li[0], 0), li[1])
        for s in ("sw", "st", "sd", "so"):
            c = self.cnt.get(s, 0)
            if c:
                waits[s] = max(waits.get(s, 0), c)
        for e in ENGINES:
            for s, t in waits.items():
                self.wt(e, s, t)

    def replay(self, block):
        for e in ENGINES:
            ops = self.q[e]
            if not ops:
                continue

            def body(eng, ops=ops):
                for f in ops:
                    f(eng)
            getattr(block, e)(body)


def build_core(dbg=False, stages=99):
    nc = bass.Bass("TRN2", target_bir_lowering=False)
    kindS = "ExternalOutput" if dbg else "Internal"

    def din(name, shape, dt=F32):
        return nc.dram_tensor(name, shape, dt, kind="ExternalInput").ap()

    x_ext = din("x_ext", [NEXT, D])
    x_oth = din("x_oth", [NOTH, D])
    wq = din("wq", [32, 128, 32, 128]); wk = din("wk", [2, 128, 32, 512])
    wv = din("wv", [2, 128, 32, 512]); wo = din("wo", [32, 128, 32, 128])
    wf = din("wf", [32, 128, 32, 128])
    la = din("la", [1, 128, 32, 128]); lb_in = din("lb", [128, 32, 128])
    w1q = din("w1q", [32, 128, 32, 128]); w1k = din("w1k", [1, 128, 32, 128])
    w1v = din("w1v", [1, 128, 32, 128]); w1o = din("w1o", [32, 128, 32, 128])
    up_w = din("up", [16, 128, 2, 1024]); dn_w = din("dn", [16, 128, 8, 256])
    qs_in = din("qs", [128, 32]); ks_in = din("ks", [128, 8]); vs_in = din("vs", [128, 8])
    os_in = din("os", [128, 32]); fb_in = din("fb", [128, 32]); lw_in = din("lw", [128, 1])
    ones_in = din("onesb", [128, 1])
    consts_in = din("consts", [128, 8])
    cosq_in = din("cosq", [128, 1024]); sinq_in = din("sinq", [128, 1024])
    cosk_in = din("cosk", [128, 1280]); sink_in = din("sink", [128, 1280])
    alph_in = din("alph", [128, 8]); omal_in = din("omal", [128, 8])
    vaug_in = din("vaug", [128, 12])
    perm_in = din("perm", [4096], U32)
    out_t = nc.dram_tensor("out", [NOWN, D], F32, kind="ExternalOutput").ap()

    def scr(name, shape, dt):
        return nc.dram_tensor(name, shape, dt, kind=kindS).ap()
    d_xnext = scr("d_xnext", [NEXT, D], BF16)
    d_xnoth = scr("d_xnoth", [NOTH, D], BF16)
    d_attn0 = scr("d_attn0", [32, NEXT, 128], BF16)
    d_ffn0 = scr("d_ffn0", [D, NEXT], BF16)
    d_attnf0 = scr("d_attnf0", [D, NEXT], BF16)
    d_x1 = scr("d_x1", [NEXT, D], F32)
    d_xn1 = scr("d_xn1", [NEXT, D], BF16)
    d_A = scr("d_A", [D, NOWN], BF16)
    d_attn1 = scr("d_attn1", [32, NOWN, 128], BF16)
    d_attnf1 = scr("d_attnf1", [D, NOWN], BF16)
    d_ffn1 = scr("d_ffn1", [D, NOWN], BF16)

    import contextlib
    with contextlib.ExitStack() as stack:
        b = Bld(nc)
        for s in ("sw", "st", "sd", "sm", "sv", "sc", "so"):
            b.sems[s] = stack.enter_context(nc.semaphore(s))

        def sb(name, shape, dt):
            return stack.enter_context(nc.sbuf_tensor(name, shape, dt))
        consts = sb("c_consts", [128, 8], F32)
        onesb = sb("c_ones", [128, 1], BF16)
        qs_t = sb("c_qs", [128, 32], F32); ks_t = sb("c_ks", [128, 8], F32)
        vs_t = sb("c_vs", [128, 8], F32); os_t = sb("c_os", [128, 32], F32)
        fb_t = sb("c_fb", [128, 32], F32); lw_t = sb("c_lw", [128, 1], F32)
        sv_pt = sb("c_svpt", [128, 8], F32)
        alph_t = sb("c_alph", [128, 8], F32); omal_t = sb("c_omal", [128, 8], F32)
        vaug_t = sb("c_vaug", [128, 12], F32)
        perm_t = sb("c_perm", [128, 32], U32)
        EPS = consts[:, 0:1]
        SCL = consts[:, 1:2]
        ONE1 = consts[0:1, 7:8]

        for dst, src in [(consts[:], consts_in), (qs_t[:], qs_in), (ks_t[:], ks_in),
                         (vs_t[:], vs_in), (os_t[:], os_in), (fb_t[:], fb_in),
                         (lw_t[:], lw_in), (alph_t[:], alph_in), (omal_t[:], omal_in),
                         (vaug_t[:], vaug_in)]:
            b.dma("sync", "st", dst, src)
        b.dma("gpsimd", "sw", onesb[:], ones_in)
        b.dma("gpsimd", "sw", perm_t[:], perm_in.rearrange("(c p) -> p c", p=128),
              allow_slow_non_contiguous=True)
        b.barrier()

        def emit_ln(xin, xout, stats, mv, stdv):
            t_bs = None
            for j in range(8):
                t_bs = b.ins("vector", lambda e, a=stats[:, j, :], x_=xin[:, j * 512:(j + 1) * 512]:
                             e.bn_stats(a, x_), "sv" if j == 7 else None)
            b.wt("vector", "sv", t_bs)
            t_ag = b.ins("vector", lambda e, a=mv[:], s=stats[:].rearrange("p a c -> p (a c)"):
                         e.bn_aggr(a, s), "sv")
            b.wt("scalar", "sv", t_ag)
            t_sq = b.ins("scalar", lambda e, a=stdv[:, 0:1], v=mv[:, 1:2], eps=EPS:
                         e.activation(out=a, in_=v, func=AF.Sqrt, bias=eps), "sc")
            b.wt("vector", "sc", t_sq)
            t_rc = b.ins("vector", lambda e, a=stdv[:, 1:2], s=stdv[:, 0:1]: e.reciprocal(a, s), "sv")
            b.wt("vector", "sv", t_rc)
            t_ln = b.ins("vector", lambda e, a=xout, x_=xin, m=mv[:, 0:1], iv=stdv[:, 1:2]:
                         e.tensor_scalar(out=a, in0=x_, scalar1=m, scalar2=iv,
                                         op0=aop.subtract, op1=aop.mult), "sv")
            return t_ln

        if stages >= 1:
            _es1 = contextlib.ExitStack()
            xt = _es1.enter_context(nc.sbuf_tensor("p1_x", [128, 2, D], F32))
            xnt = _es1.enter_context(nc.sbuf_tensor("p1_xn", [128, 2, D], BF16))
            stats = _es1.enter_context(nc.sbuf_tensor("p1_st", [128, 8, 6], F32))
            mv = _es1.enter_context(nc.sbuf_tensor("p1_mv", [128, 2], F32))
            stdv = _es1.enter_context(nc.sbuf_tensor("p1_sd", [128, 2], F32))
            if True:
                tiles = [(x_ext[t * 128:(t + 1) * 128, :], d_xnext[t * 128:(t + 1) * 128, :])
                         for t in range(10)]
                tiles += [(x_oth[t * 128:(t + 1) * 128, :], d_xnoth[t * 128:(t + 1) * 128, :])
                          for t in range(6)]
                prev = {}
                for i, (src, dst) in enumerate(tiles):
                    bi = i % 2
                    pg = prev.get(bi)
                    if pg is not None:
                        b.wt("sync", "sd", pg)
                    t_ld = b.dma("sync", "st", xt[:, bi, :], src)
                    b.wt("vector", "st", t_ld)
                    if pg is not None:
                        b.wt("vector", "sd", pg)
                    t_ln = emit_ln(xt[:, bi, :], xnt[:, bi, :], stats, mv, stdv)
                    b.wt("scalar", "sv", t_ln)
                    t_st = b.dma("scalar", "sd", dst, xnt[:, bi, :])
                    prev[bi] = t_st
                b.barrier()
            _es1.close()

        kv_stack = contextlib.ExitStack()
        K_fm = kv_stack.enter_context(nc.sbuf_tensor("K_fm", [128, 8, 2048], BF16))
        V_ag = kv_stack.enter_context(nc.sbuf_tensor("V_ag", [128, 8, 16, 130], BF16))
        if stages >= 2:
            _es2 = contextlib.ExitStack()
            xnh = _es2.enter_context(nc.sbuf_tensor("p2_xnh", [128, 32, 1024], BF16))
            wraw = _es2.enter_context(nc.sbuf_tensor("p2_raw", [128, 32, 512], BF16))
            wabs = _es2.enter_context(nc.sbuf_tensor("p2_abs", [128, 32, 128], BF16))
            wsgn = _es2.enter_context(nc.sbuf_tensor("p2_sgn", [128, 32, 512], BF16))
            csr = _es2.enter_context(nc.sbuf_tensor("p2_csr", [1, 512], F32))
            spt = _es2.enter_context(nc.sbuf_tensor("p2_spt", [128, 8], F32))
            psA = _es2.enter_context(nc.psum_tensor("p2_psA", [128, 512], F32))
            psB = _es2.enter_context(nc.psum_tensor("p2_psB", [128, 512], F32))
            csp_f = _es2.enter_context(nc.psum_tensor("p2_csp", [128, 512], F32))
            tps = _es2.enter_context(nc.psum_tensor("p2_tp", [128, 512], F32))
            if True:
                pool = PsumPool([psA, psB])
                t_aug0 = b.ins("vector", lambda e, a=V_ag[:, :, :, 128:129]: e.memset(a, 1.0), "sv")

                def stats_block(ob, dst_spt, scale_t):
                    last_cs = None
                    t_cm = None
                    for sub in range(4):
                        if t_cm is not None:
                            b.wt("scalar", "sm", t_cm)
                        t_ab = b.ins("scalar", lambda e, a=wabs[:],
                                     w=wraw[:, :, sub * 128:(sub + 1) * 128]:
                                     e.activation(out=a, in_=w, func=AF.Abs), "sc")
                        b.wt("tensor", "sc", t_ab)
                        if last_cs is not None:
                            b.wt("tensor", "sv", last_cs)
                        for dc in range(32):
                            t_cm = b.ins("tensor", lambda e, o=csp_f[0:1, sub * 128:(sub + 1) * 128],
                                         l=onesb[:], r=wabs[:, dc, :], s=(dc == 0), st_=(dc == 31):
                                         e.matmul(o, l, r, start=s, stop=st_),
                                         "sm" if dc == 31 else None)
                        b.wt("vector", "sm", t_cm)
                        last_cs = b.ins("vector", lambda e, a=csr[0:1, sub * 128:(sub + 1) * 128],
                                        p=csp_f[0:1, sub * 128:(sub + 1) * 128]: e.tensor_copy(a, p), "sv")
                    b.wt("tensor", "sv", last_cs)
                    for sub in range(4):
                        t_tm = b.ins("tensor", lambda e, o=tps[:, sub:sub + 1],
                                     l=csr[0:1, sub * 128:(sub + 1) * 128], r=ONE1:
                                     e.matmul(o, l, r, start=True, stop=True),
                                     "sm" if sub == 3 else None)
                    b.wt("vector", "sm", t_tm)
                    t_s = b.ins("vector", lambda e, a=dst_spt[:, ob * 4:(ob + 1) * 4], p=tps[:, 0:4],
                                sc=scale_t[:, ob * 4:(ob + 1) * 4]:
                                e.scalar_tensor_tensor(out=a, in0=p, scalar=1.0 / 4096.0, in1=sc,
                                                       op0=aop.mult, op1=aop.mult), "sv")
                    return t_s

                last_wsgn_use = None
                last_wraw_use = None
                t_sk = {}
                t_sv_ = {}
                for half in range(2):
                    if half == 1:
                        b.wt("sync", "sm", b.cnt.get("sm", 0))
                    srcs = ([(d_xnext[0:1024, :], 0, 1024)] if half == 0 else
                            [(d_xnext[1024:1280, :], 0, 256), (d_xnoth, 256, 768)])
                    for dc in range(32):
                        for (srct, c0, w_) in srcs:
                            t_xn = b.dmat(xnh[:, dc, c0:c0 + w_], srct[:, dc * 128:(dc + 1) * 128])
                    b.wt("tensor", "st", t_xn)

                    for mat in ("k", "v"):
                        wimg = wk if mat == "k" else wv
                        for ob in range(2):
                            if last_wraw_use is not None:
                                b.wt("gpsimd", "sc", last_wraw_use)
                            t_w = b.dma("gpsimd", "sw", wraw[:].rearrange("p c o -> p (c o)"),
                                        wimg[ob].rearrange("p c o -> p (c o)"))
                            b.wt("vector", "sw", t_w)
                            b.wt("scalar", "sw", t_w)
                            if half == 0:
                                tgt = stats_block(ob, spt if mat == "k" else sv_pt,
                                                  ks_t if mat == "k" else vs_t)
                                if mat == "k":
                                    t_sk[ob] = tgt
                                else:
                                    t_sv_[ob] = tgt

                            if last_wsgn_use is not None:
                                b.wt("scalar", "sm", last_wsgn_use)
                            t_sg = b.ins("scalar", lambda e, a=wsgn[:].rearrange("p c o -> p (c o)"),
                                         w=wraw[:].rearrange("p c o -> p (c o)"):
                                         e.activation(out=a, in_=w, func=AF.Sign), "sc")
                            last_wraw_use = t_sg
                            b.wt("tensor", "sc", t_sg)
                            if mat == "k":
                                b.wt("scalar", "sv", t_sk[ob])
                                for oc in range(4):
                                    for tb in range(2):
                                        bi = (oc * 2 + tb) % 2
                                        ps = pool.acq(b, bi)
                                        for dc in range(32):
                                            t_mm = b.ins("tensor", lambda e, o=ps[:],
                                                         l=wsgn[:, dc, oc * 128:(oc + 1) * 128],
                                                         r=xnh[:, dc, tb * 512:(tb + 1) * 512],
                                                         s=(dc == 0), st_=(dc == 31):
                                                         e.matmul(o, l, r, start=s, stop=st_),
                                                         "sm" if dc == 31 else None)
                                        last_wsgn_use = t_mm
                                        b.wt("scalar", "sm", t_mm)
                                        t_ev = b.ins("scalar", lambda e,
                                                     a=K_fm[:, ob * 4 + oc,
                                                            half * 1024 + tb * 512:half * 1024 + (tb + 1) * 512],
                                                     p=ps[:], s=spt[:, ob * 4 + oc:ob * 4 + oc + 1]:
                                                     e.activation(out=a, in_=p, func=AF.Copy, scale=s), "sc")
                                        pool.rel(bi, "sc", t_ev)
                            else:
                                b.wt("vector", "sv", t_sv_[ob])
                                b.wt("vector", "sv", t_aug0)
                                for tt in range(8):
                                    bi = tt % 2
                                    ps = pool.acq(b, bi)
                                    for dc in range(32):
                                        t_mm = b.ins("tensor", lambda e, o=ps[:],
                                                     l=xnh[:, dc, tt * 128:(tt + 1) * 128],
                                                     r=wsgn[:, dc, :],
                                                     s=(dc == 0), st_=(dc == 31):
                                                     e.matmul(o, l, r, start=s, stop=st_),
                                                     "sm" if dc == 31 else None)
                                    last_wsgn_use = t_mm
                                    b.wt("vector", "sm", t_mm)
                                    kc_g = half * 8 + tt
                                    for sh in range(4):
                                        t_ev = b.ins("vector", lambda e,
                                                     a=V_ag[:, ob * 4 + sh, kc_g, 0:128],
                                                     p=ps[:, sh * 128:(sh + 1) * 128],
                                                     sc=sv_pt[:, ob * 4 + sh:ob * 4 + sh + 1]:
                                                     e.tensor_scalar(out=a, in0=p, scalar1=sc,
                                                                     scalar2=None, op0=aop.mult), "sv")
                                    pool.rel(bi, "sv", t_ev)
                b.barrier()
                if dbg:
                    dK = scr("dbg_K", [128, 8, 2048], BF16)
                    dV = scr("dbg_V", [128, 8, 16, 130], BF16)
                    dsv = scr("dbg_sv", [128, 8], F32)
                    b.dma("scalar", "sd", dK, K_fm[:])
                    b.dma("scalar", "sd", dV, V_ag[:])
                    t_d = b.dma("scalar", "sd", dsv, sv_pt[:])
                    b.wt("scalar", "sd", t_d)
                    b.barrier()
            _es2.close()

        p34_stack = contextlib.ExitStack()
        xn_e = p34_stack.enter_context(nc.sbuf_tensor("xn_e", [128, 32, 1280], BF16, side="right"))
        if stages >= 3:
            _es3 = contextlib.ExitStack()
            wraw = _es3.enter_context(nc.sbuf_tensor("p3_raw", [128, 2, 32, 128], BF16))
            wabs = _es3.enter_context(nc.sbuf_tensor("p3_abs", [128, 32, 128], BF16))
            wsgn = _es3.enter_context(nc.sbuf_tensor("p3_sgn", [128, 2, 32, 128], BF16))
            csr = _es3.enter_context(nc.sbuf_tensor("p3_csr", [1, 128], F32))
            sq_t = _es3.enter_context(nc.sbuf_tensor("p3_sq", [128, 2], F32))
            q_fm = _es3.enter_context(nc.sbuf_tensor("p3_qf", [128, 2, 512], BF16))
            probsT = _es3.enter_context(nc.sbuf_tensor("p3_pT", [128, 8, 512], BF16))
            rc_t = _es3.enter_context(nc.sbuf_tensor("p3_rc", [128, 4], F32))
            stage = _es3.enter_context(nc.sbuf_tensor("p3_stg", [128, 2, 4, 128], BF16))
            q0 = _es3.enter_context(nc.psum_tensor("p3_q0", [128, 512], F32))
            q1 = _es3.enter_context(nc.psum_tensor("p3_q1", [128, 512], F32))
            s0 = _es3.enter_context(nc.psum_tensor("p3_s0", [128, 512], F32))
            s1 = _es3.enter_context(nc.psum_tensor("p3_s1", [128, 512], F32))
            a0 = _es3.enter_context(nc.psum_tensor("p3_a0", [128, 512], F32))
            a1 = _es3.enter_context(nc.psum_tensor("p3_a1", [128, 512], F32))
            a2 = _es3.enter_context(nc.psum_tensor("p3_a2", [128, 512], F32))
            a3 = _es3.enter_context(nc.psum_tensor("p3_a3", [128, 512], F32))
            if True:
                qpool = PsumPool([q0, q1])
                spool = PsumPool([s0, s1])
                apool = PsumPool([a0, a1, a2, a3])
                for dc in range(32):
                    t_xe = b.dmat(xn_e[:, dc, :], d_xnext[:, dc * 128:(dc + 1) * 128])
                b.wt("tensor", "st", t_xe)

                pT_guard = [None, None, None, None]
                w_guard = [None, None]
                sq_guard = [None, None]
                qf_guard = [None, None]
                stage_guard = [None, None]
                for h in range(32):
                    kv = h // 4
                    wb = h % 2
                    if w_guard[wb] is not None:
                        b.wt("gpsimd", w_guard[wb][0], w_guard[wb][1])
                    t_w = b.dma("gpsimd", "sw", wraw[:, wb, :, :].rearrange("p c o -> p (c o)"),
                                wq[h].rearrange("p c o -> p (c o)"))
                    b.wt("scalar", "sw", t_w)
                    t_ab = b.ins("scalar", lambda e, a=wabs[:], w=wraw[:, wb, :, :]:
                                 e.activation(out=a, in_=w, func=AF.Abs), "sc")
                    ps_cs = qpool.acq(b, 0)
                    b.wt("tensor", "sc", t_ab)
                    for dc in range(32):
                        t_cm = b.ins("tensor", lambda e, o=ps_cs[0:1, 0:128], l=onesb[:],
                                     r=wabs[:, dc, :], s=(dc == 0), st_=(dc == 31):
                                     e.matmul(o, l, r, start=s, stop=st_),
                                     "sm" if dc == 31 else None)
                    b.wt("vector", "sm", t_cm)
                    t_cr = b.ins("vector", lambda e, a=csr[:], p=ps_cs[0:1, 0:128]:
                                 e.tensor_copy(a, p), "sv")
                    b.wt("tensor", "sv", t_cr)
                    t_tm = b.ins("tensor", lambda e, o=ps_cs[0:128, 128:129], l=csr[:], r=ONE1:
                                 e.matmul(o, l, r, start=True, stop=True), "sm")
                    if sq_guard[wb] is not None:
                        b.wt("vector", sq_guard[wb][0], sq_guard[wb][1])
                    b.wt("vector", "sm", t_tm)
                    t_sq = b.ins("vector", lambda e, a=sq_t[:, wb:wb + 1], p=ps_cs[0:128, 128:129],
                                 sc=qs_t[:, h:h + 1]:
                                 e.scalar_tensor_tensor(out=a, in0=p, scalar=1.0 / 4096.0, in1=sc,
                                                        op0=aop.mult, op1=aop.mult), "sv")
                    qpool.rel(0, "sv", t_sq)
                    b.wt("scalar", "sw", t_w)
                    t_sg = b.ins("scalar", lambda e, a=wsgn[:, wb, :, :].rearrange("p c o -> p (c o)"),
                                 w=wraw[:, wb, :, :].rearrange("p c o -> p (c o)"):
                                 e.activation(out=a, in_=w, func=AF.Sign), "sc")
                    w_guard[wb] = ("sc", t_sg)
                    b.wt("tensor", "sc", t_sg)

                    t_qf_last = None
                    for qb, (qo, qw) in enumerate(QBS):
                        nt = qw // 128
                        ps_q = qpool.acq(b, qb % 2)
                        for dc in range(32):
                            t_qm = b.ins("tensor", lambda e, o=ps_q[:, 0:qw],
                                         l=wsgn[:, wb, dc, :], r=xn_e[:, dc, qo:qo + qw],
                                         s=(dc == 0), st_=(dc == 31):
                                         e.matmul(o, l, r, start=s, stop=st_),
                                         "sm" if dc == 31 else None)
                        b.wt("scalar", "sm", t_qm)
                        b.wt("scalar", "sv", t_sq)
                        if qf_guard[qb % 2] is not None:
                            b.wt("scalar", qf_guard[qb % 2][0], qf_guard[qb % 2][1])
                        t_qf = b.ins("scalar", lambda e, a=q_fm[:, qb % 2, 0:qw], p=ps_q[:, 0:qw],
                                     s=sq_t[:, wb:wb + 1]:
                                     e.activation(out=a, in_=p, func=AF.Copy, scale=s), "sc")
                        t_qf_last = t_qf
                        qpool.rel(qb % 2, "sc", t_qf)
                        b.wt("tensor", "sc", t_qf)

                        t_av_last = None
                        t_sc_last = None
                        for g in range(8):
                            for j in range(2):
                                kc = 2 * g + j
                                ps_s = spool.acq(b, kc % 2)
                                t_sc = b.ins("tensor", lambda e, o=ps_s[:, 0:qw],
                                             l=K_fm[:, kv, kc * 128:(kc + 1) * 128],
                                             r=q_fm[:, qb % 2, 0:qw]:
                                             e.matmul(o, l, r, start=True, stop=True),
                                             "sm" if j == 1 else None)
                            t_sc_last = t_sc
                            b.wt("scalar", "sm", t_sc)
                            if pT_guard[g % 4] is not None:
                                b.wt("scalar", pT_guard[g % 4][0], pT_guard[g % 4][1])
                            for j in range(2):
                                kc = 2 * g + j
                                t_ex = b.ins("scalar", lambda e, a=probsT[:, kc % 8, 0:qw],
                                             p=spool.t[kc % 2][:, 0:qw], s=SCL:
                                             e.activation(out=a, in_=p, func=AF.Exp, scale=s),
                                             "sc" if j == 1 else None)
                            spool.rel(0, "sc", t_ex)
                            spool.rel(1, "sc", t_ex)
                            b.wt("tensor", "sc", t_ex)
                            for j in range(2):
                                kc = 2 * g + j
                                for jt in range(nt):
                                    if g == 0 and j == 0:
                                        apool.acq(b, jt)
                                    t_av = b.ins("tensor", lambda e, o=apool.t[jt][:, 0:129],
                                                 l=probsT[:, kc % 8, jt * 128:(jt + 1) * 128],
                                                 r=V_ag[:, kv, kc, 0:129],
                                                 s=(kc == 0), st_=(kc == 15):
                                                 e.matmul(o, l, r, start=s, stop=st_),
                                                 "sm" if (j == 1 and jt == nt - 1) else None)
                            pT_guard[g % 4] = ("sm", t_av)
                            t_av_last = t_av
                        qf_guard[qb % 2] = ("sm", t_sc_last)
                        b.wt("vector", "sm", t_av_last)
                        sb_ = qb % 2
                        if stage_guard[sb_] is not None:
                            b.wt("vector", stage_guard[sb_][0], stage_guard[sb_][1])
                        for jt in range(nt):
                            t_rc = b.ins("vector", lambda e, a=rc_t[:, jt:jt + 1], p=apool.t[jt][:, 128:129]:
                                         e.reciprocal(a, p), "sv")
                        b.wt("vector", "sv", t_rc)
                        for jt in range(nt):
                            t_st = b.ins("vector", lambda e, a=stage[:, sb_, jt, :],
                                         p=apool.t[jt][:, 0:128], r=rc_t[:, jt:jt + 1]:
                                         e.tensor_scalar(out=a, in0=p, scalar1=r, scalar2=None, op0=aop.mult), "sv")
                            apool.rel(jt, "sv", t_st)
                        b.wt("scalar", "sv", t_st)
                        t_sp = b.dma("scalar", "sd",
                                     d_attn0[h][qo:qo + qw, :].rearrange("(t p) d -> p t d", p=128),
                                     stage[:, sb_, 0:nt, :])
                        stage_guard[sb_] = ("sd", t_sp)
                    sq_guard[wb] = ("sc", t_qf_last)
                b.barrier()
            _es3.close()
        kv_stack.close()

        if stages >= 4:
            _es4 = contextlib.ExitStack()
            wraw = _es4.enter_context(nc.sbuf_tensor("p4_raw", [128, 2, 32, 128], BF16))
            la_t = _es4.enter_context(nc.sbuf_tensor("p4_la", [128, 32, 128], BF16))
            lb_t = _es4.enter_context(nc.sbuf_tensor("p4_lb", [128, 32, 128], BF16))
            hall = _es4.enter_context(nc.sbuf_tensor("p4_h", [128, 1280], BF16))
            stage = _es4.enter_context(nc.sbuf_tensor("p4_stg", [128, 2, 512], BF16))
            f0 = _es4.enter_context(nc.psum_tensor("p4_p0", [128, 512], F32))
            f1 = _es4.enter_context(nc.psum_tensor("p4_p1", [128, 512], F32))
            if True:
                fpool = PsumPool([f0, f1])
                t_la = b.dma("gpsimd", "sw", la_t[:].rearrange("p c o -> p (c o)"),
                             la[0].rearrange("p c o -> p (c o)"))
                t_lb = b.dma("gpsimd", "sw", lb_t[:].rearrange("p c o -> p (c o)"),
                             lb_in.rearrange("p c o -> p (c o)"))
                b.wt("vector", "sw", t_lb)
                t_lbs = b.ins("vector", lambda e, a=lb_t[:].rearrange("p c o -> p (c o)"),
                              w=lw_t[:, 0:1]:
                              e.tensor_scalar(out=a, in0=a, scalar1=w, scalar2=2.0,
                                              op0=aop.mult, op1=aop.mult), "sv")
                b.wt("tensor", "sw", t_la)
                t_he = None
                for qb, (qo, qw) in enumerate(QBS):
                    ps = fpool.acq(b, 0)
                    for dc in range(32):
                        t_hm = b.ins("tensor", lambda e, o=ps[:, 0:qw], l=la_t[:, dc, :],
                                     r=xn_e[:, dc, qo:qo + qw], s=(dc == 0), st_=(dc == 31):
                                     e.matmul(o, l, r, start=s, stop=st_),
                                     "sm" if dc == 31 else None)
                    b.wt("vector", "sm", t_hm)
                    t_he = b.ins("vector", lambda e, a=hall[:, qo:qo + qw], p=ps[:, 0:qw]:
                                 e.tensor_copy(a, p), "sv")
                    fpool.rel(0, "sv", t_he)
                b.wt("tensor", "sv", t_lbs)
                b.wt("tensor", "sv", t_he)

                w_guard = [None, None]
                stage_guard = [None, None]
                for ob in range(32):
                    wb = ob % 2
                    if w_guard[wb] is not None:
                        b.wt("gpsimd", w_guard[wb][0], w_guard[wb][1])
                    t_w = b.dma("gpsimd", "sw", wraw[:, wb, :, :].rearrange("p c o -> p (c o)"),
                                wf[ob].rearrange("p c o -> p (c o)"))
                    b.wt("tensor", "sw", t_w)
                    for qb, (qo, qw) in enumerate(QBS):
                        ps = fpool.acq(b, qb % 2)
                        for dc in range(32):
                            b.ins("tensor", lambda e, o=ps[:, 0:qw], l=wraw[:, wb, dc, :],
                                  r=xn_e[:, dc, qo:qo + qw], s=(dc == 0):
                                  e.matmul(o, l, r, start=s, stop=False))
                        t_fm = b.ins("tensor", lambda e, o=ps[:, 0:qw], l=lb_t[:, ob, :],
                                     r=hall[:, qo:qo + qw]:
                                     e.matmul(o, l, r, start=False, stop=True), "sm")
                        if qb == 2:
                            w_guard[wb] = ("sm", t_fm)
                        b.wt("scalar", "sm", t_fm)
                        sb_ = qb % 2
                        if stage_guard[sb_] is not None:
                            b.wt("scalar", stage_guard[sb_][0], stage_guard[sb_][1])
                        t_ev = b.ins("scalar", lambda e, a=stage[:, sb_, 0:qw], p=ps[:, 0:qw],
                                     bi=fb_t[:, ob:ob + 1]:
                                     e.activation(out=a, in_=p, func=AF.Identity, bias=bi), "sc")
                        fpool.rel(qb % 2, "sc", t_ev)
                        b.wt("scalar", "sc", t_ev)
                        t_sp = b.dma("scalar", "sd", d_ffn0[ob * 128:(ob + 1) * 128, qo:qo + qw],
                                     stage[:, sb_, 0:qw])
                        stage_guard[sb_] = ("sd", t_sp)
                b.barrier()
            _es4.close()
        p34_stack.close()

        if stages >= 5:
            _es5 = contextlib.ExitStack()
            at_fm = _es5.enter_context(nc.sbuf_tensor("p5_att", [128, 32, 1280], BF16))
            wraw = _es5.enter_context(nc.sbuf_tensor("p5_raw", [128, 2, 32, 128], BF16))
            wabs = _es5.enter_context(nc.sbuf_tensor("p5_abs", [128, 32, 128], BF16))
            wsgn = _es5.enter_context(nc.sbuf_tensor("p5_sgn", [128, 2, 32, 128], BF16))
            csr = _es5.enter_context(nc.sbuf_tensor("p5_csr", [1, 128], F32))
            so_t = _es5.enter_context(nc.sbuf_tensor("p5_so", [128, 2], F32))
            stage = _es5.enter_context(nc.sbuf_tensor("p5_stg", [128, 2, 512], BF16))
            o0 = _es5.enter_context(nc.psum_tensor("p5_p0", [128, 512], F32))
            o1 = _es5.enter_context(nc.psum_tensor("p5_p1", [128, 512], F32))
            ocs = _es5.enter_context(nc.psum_tensor("p5_cs", [128, 512], F32))
            if True:
                opool = PsumPool([o0, o1])
                for ac in range(32):
                    t_at = b.dmat(at_fm[:, ac, :], d_attn0[ac])
                b.wt("tensor", "st", t_at)
                w_guard = [None, None]
                so_guard = [None, None]
                stage_guard = [None, None]
                cs_guard = None
                for ob in range(32):
                    wb = ob % 2
                    if w_guard[wb] is not None:
                        b.wt("gpsimd", w_guard[wb][0], w_guard[wb][1])
                    t_w = b.dma("gpsimd", "sw", wraw[:, wb, :, :].rearrange("p c o -> p (c o)"),
                                wo[ob].rearrange("p c o -> p (c o)"))
                    b.wt("scalar", "sw", t_w)
                    t_ab = b.ins("scalar", lambda e, a=wabs[:], w=wraw[:, wb, :, :]:
                                 e.activation(out=a, in_=w, func=AF.Abs), "sc")
                    b.wt("tensor", "sc", t_ab)
                    if cs_guard is not None:
                        b.wt("tensor", cs_guard[0], cs_guard[1])
                    for dc in range(32):
                        t_cm = b.ins("tensor", lambda e, o=ocs[0:1, 0:128], l=onesb[:],
                                     r=wabs[:, dc, :], s=(dc == 0), st_=(dc == 31):
                                     e.matmul(o, l, r, start=s, stop=st_),
                                     "sm" if dc == 31 else None)
                    b.wt("vector", "sm", t_cm)
                    t_cr = b.ins("vector", lambda e, a=csr[:], p=ocs[0:1, 0:128]:
                                 e.tensor_copy(a, p), "sv")
                    b.wt("tensor", "sv", t_cr)
                    t_tm = b.ins("tensor", lambda e, o=ocs[0:128, 128:129], l=csr[:], r=ONE1:
                                 e.matmul(o, l, r, start=True, stop=True), "sm")
                    if so_guard[wb] is not None:
                        b.wt("vector", so_guard[wb][0], so_guard[wb][1])
                    b.wt("vector", "sm", t_tm)
                    t_so = b.ins("vector", lambda e, a=so_t[:, wb:wb + 1], p=ocs[0:128, 128:129],
                                 sc=os_t[:, ob:ob + 1]:
                                 e.scalar_tensor_tensor(out=a, in0=p, scalar=1.0 / 4096.0, in1=sc,
                                                        op0=aop.mult, op1=aop.mult), "sv")
                    cs_guard = ("sv", t_so)
                    b.wt("scalar", "sw", t_w)
                    t_sg = b.ins("scalar", lambda e, a=wsgn[:, wb, :, :].rearrange("p c o -> p (c o)"),
                                 w=wraw[:, wb, :, :].rearrange("p c o -> p (c o)"):
                                 e.activation(out=a, in_=w, func=AF.Sign), "sc")
                    w_guard[wb] = ("sc", t_sg)
                    b.wt("tensor", "sc", t_sg)
                    t_ev_last = None
                    for qb, (qo, qw) in enumerate(QBS):
                        ps = opool.acq(b, qb % 2)
                        for ac in range(32):
                            t_om = b.ins("tensor", lambda e, o=ps[:, 0:qw], l=wsgn[:, wb, ac, :],
                                         r=at_fm[:, ac, qo:qo + qw], s=(ac == 0), st_=(ac == 31):
                                         e.matmul(o, l, r, start=s, stop=st_),
                                         "sm" if ac == 31 else None)
                        b.wt("scalar", "sm", t_om)
                        b.wt("scalar", "sv", t_so)
                        sb_ = qb % 2
                        if stage_guard[sb_] is not None:
                            b.wt("scalar", stage_guard[sb_][0], stage_guard[sb_][1])
                        t_ev = b.ins("scalar", lambda e, a=stage[:, sb_, 0:qw], p=ps[:, 0:qw],
                                     s=so_t[:, wb:wb + 1]:
                                     e.activation(out=a, in_=p, func=AF.Copy, scale=s), "sc")
                        t_ev_last = t_ev
                        opool.rel(qb % 2, "sc", t_ev)
                        b.wt("scalar", "sc", t_ev)
                        t_sp = b.dma("scalar", "sd", d_attnf0[ob * 128:(ob + 1) * 128, qo:qo + qw],
                                     stage[:, sb_, 0:qw])
                        stage_guard[sb_] = ("sd", t_sp)
                    so_guard[wb] = ("sc", t_ev_last)
                b.barrier()
            _es5.close()

        if stages >= 6:
            _es6 = contextlib.ExitStack()
            xt = _es6.enter_context(nc.sbuf_tensor("p6_x", [128, 2, D], F32))
            att = _es6.enter_context(nc.sbuf_tensor("p6_at", [128, 2, D], BF16))
            fft = _es6.enter_context(nc.sbuf_tensor("p6_ff", [128, 2, D], BF16))
            x1t = _es6.enter_context(nc.sbuf_tensor("p6_x1", [128, 2, D], F32))
            xnt = _es6.enter_context(nc.sbuf_tensor("p6_xn", [128, 2, D], BF16))
            stats = _es6.enter_context(nc.sbuf_tensor("p6_st", [128, 8, 6], F32))
            mv = _es6.enter_context(nc.sbuf_tensor("p6_mv", [128, 2], F32))
            stdv = _es6.enter_context(nc.sbuf_tensor("p6_sd", [128, 2], F32))
            if True:
                prev = {}
                for tt in range(10):
                    bi = tt % 2
                    pg = prev.get(bi)
                    if pg is not None:
                        b.wt("sync", "sd", pg)
                    b.dma("sync", "st", xt[:, bi, :], x_ext[tt * 128:(tt + 1) * 128, :])
                    b.dmat(att[:, bi, :], d_attnf0[:, tt * 128:(tt + 1) * 128])
                    t_f = b.dmat(fft[:, bi, :], d_ffn0[:, tt * 128:(tt + 1) * 128])
                    b.wt("vector", "st", t_f)
                    if pg is not None:
                        b.wt("vector", "sd", pg)
                    t_a1 = b.ins("vector", lambda e, a=x1t[:, bi, :], i0=att[:, bi, :], i1=fft[:, bi, :]:
                                 e.tensor_tensor(out=a, in0=i0, in1=i1, op=aop.add), "sv")
                    b.wt("vector", "sv", t_a1)
                    t_a2 = b.ins("vector", lambda e, a=x1t[:, bi, :], i1=xt[:, bi, :]:
                                 e.tensor_tensor(out=a, in0=a, in1=i1, op=aop.add), "sv")
                    b.wt("vector", "sv", t_a2)
                    t_ln = emit_ln(x1t[:, bi, :], xnt[:, bi, :], stats, mv, stdv)
                    b.wt("scalar", "sv", t_ln)
                    b.dma("scalar", "sd", d_x1[tt * 128:(tt + 1) * 128, :], x1t[:, bi, :])
                    t_s2 = b.dma("scalar", "sd", d_xn1[tt * 128:(tt + 1) * 128, :], xnt[:, bi, :])
                    prev[bi] = t_s2
                b.barrier()
            _es6.close()

        p78_stack = contextlib.ExitStack()
        xn1 = p78_stack.enter_context(nc.sbuf_tensor("xn1", [128, 32, 1280], BF16, side="right"))
        Kr1 = p78_stack.enter_context(nc.sbuf_tensor("Kr1", [128, 12, 128], BF16, side="right"))
        V1g = p78_stack.enter_context(nc.sbuf_tensor("V1g", [128, 12, 130], BF16, side="right"))
        cosq_t = p78_stack.enter_context(nc.sbuf_tensor("c_cosq", [128, 1024], F32, side="right"))
        sinq_t = p78_stack.enter_context(nc.sbuf_tensor("c_sinq", [128, 1024], F32, side="right"))
        if stages >= 7:
            _es7 = contextlib.ExitStack()
            w1k_t = _es7.enter_context(nc.sbuf_tensor("p7_w1k", [128, 32, 128], BF16))
            w1v_t = _es7.enter_context(nc.sbuf_tensor("p7_w1v", [128, 32, 128], BF16))
            cosk_t = _es7.enter_context(nc.sbuf_tensor("p7_cosk", [128, 1280], F32))
            sink_t = _es7.enter_context(nc.sbuf_tensor("p7_sink", [128, 1280], F32))
            k_raw = _es7.enter_context(nc.sbuf_tensor("p7_kr", [128, 1280], BF16))
            k_swp = _es7.enter_context(nc.sbuf_tensor("p7_ks", [128, 1280], BF16))
            k_cos = _es7.enter_context(nc.sbuf_tensor("p7_kc", [128, 1280], BF16))
            k0 = _es7.enter_context(nc.psum_tensor("p7_p0", [128, 512], F32))
            k1 = _es7.enter_context(nc.psum_tensor("p7_p1", [128, 512], F32))
            if True:
                kpool = PsumPool([k0, k1])
                b.dma("sync", "st", cosq_t[:], cosq_in)
                b.dma("sync", "st", sinq_t[:], sinq_in)
                b.dma("sync", "st", cosk_t[:], cosk_in)
                b.dma("sync", "st", sink_t[:], sink_in)
                for dc in range(32):
                    t_x1 = b.dmat(xn1[:, dc, :], d_xn1[:, dc * 128:(dc + 1) * 128])
                    b.wt("scalar", "st", t_x1)
                    b.dma("scalar", "sd", d_A[dc * 128:(dc + 1) * 128, :], xn1[:, dc, 0:1024])
                b.wt("tensor", "st", t_x1)
                t_k = b.dma("gpsimd", "sw", w1k_t[:].rearrange("p c o -> p (c o)"),
                            w1k[0].rearrange("p c o -> p (c o)"))
                t_v = b.dma("gpsimd", "sw", w1v_t[:].rearrange("p c o -> p (c o)"),
                            w1v[0].rearrange("p c o -> p (c o)"))
                t_aug = b.ins("vector", lambda e, a=V1g[:, :, 128:129], v=vaug_t[:].unsqueeze(2):
                              e.tensor_copy(a, v), "sv")
                b.wt("tensor", "sw", t_k)
                t_ke = None
                for qb, (qo, qw) in enumerate(QBS):
                    ps = kpool.acq(b, qb % 2)
                    for dc in range(32):
                        t_km = b.ins("tensor", lambda e, o=ps[:, 0:qw], l=w1k_t[:, dc, :],
                                     r=xn1[:, dc, qo:qo + qw], s=(dc == 0), st_=(dc == 31):
                                     e.matmul(o, l, r, start=s, stop=st_),
                                     "sm" if dc == 31 else None)
                    b.wt("vector", "sm", t_km)
                    t_ke = b.ins("vector", lambda e, a=k_raw[:, qo:qo + qw], p=ps[:, 0:qw]:
                                 e.tensor_copy(a, p), "sv")
                    kpool.rel(qb % 2, "sv", t_ke)
                b.wt("sync", "sv", t_ke)
                b.dma("sync", "st", k_swp[0:64, :], k_raw[64:128, :])
                t_sw = b.dma("sync", "st", k_swp[64:128, :], k_raw[0:64, :])
                b.wt("vector", "st", t_sw)
                b.ins("vector", lambda e, a=k_cos[:], i0=k_raw[:], i1=cosk_t[:]:
                      e.tensor_tensor(out=a, in0=i0, in1=i1, op=aop.mult))
                t_r2 = b.ins("vector", lambda e, a=k_swp[:], i1=sink_t[:]:
                             e.tensor_tensor(out=a, in0=a, in1=i1, op=aop.mult), "sv")
                b.wt("vector", "sv", t_r2)
                b.ins("vector", lambda e, a=Kr1[:, 2:10, :].rearrange("p c k -> p (c k)"),
                      i0=k_cos[:, 0:1024], i1=k_swp[:, 0:1024]:
                      e.tensor_tensor(out=a, in0=i0, in1=i1, op=aop.add))
                b.ins("vector", lambda e, a=Kr1[:, 0:2, :].rearrange("p c k -> p (c k)"),
                      i0=k_cos[:, 1024:1280], i1=k_swp[:, 1024:1280]:
                      e.tensor_tensor(out=a, in0=i0, in1=i1, op=aop.add))
                b.ins("vector", lambda e, a=Kr1[:, 10:12, :].rearrange("p c k -> p (c k)"),
                      i0=k_cos[:, 1024:1280], i1=k_swp[:, 1024:1280]:
                      e.tensor_tensor(out=a, in0=i0, in1=i1, op=aop.add))
                b.wt("tensor", "sw", t_v)
                for tt in range(10):
                    ps = kpool.acq(b, tt % 2)
                    for dc in range(32):
                        t_vm = b.ins("tensor", lambda e, o=ps[:, 0:128],
                                     l=xn1[:, dc, tt * 128:(tt + 1) * 128],
                                     r=w1v_t[:, dc, :], s=(dc == 0), st_=(dc == 31):
                                     e.matmul(o, l, r, start=s, stop=st_),
                                     "sm" if dc == 31 else None)
                    b.wt("vector", "sm", t_vm)
                    dsts = [2 + tt] if tt < 8 else [tt - 8, tt + 2]
                    t_ve = None
                    for ch in dsts:
                        t_ve = b.ins("vector", lambda e, a=V1g[:, ch, 0:128], p=ps[:, 0:128],
                                     s=vaug_t[:, ch:ch + 1]:
                                     e.tensor_scalar(out=a, in0=p, scalar1=s, scalar2=None, op0=aop.mult), "sv")
                    kpool.rel(tt % 2, "sv", t_ve)
                b.barrier()
            _es7.close()

        if stages >= 8:
            _es8 = contextlib.ExitStack()
            w1q_t = _es8.enter_context(nc.sbuf_tensor("p8_w1q", [128, 2, 32, 128], BF16))
            q_raw = _es8.enter_context(nc.sbuf_tensor("p8_qr", [128, 1024], BF16))
            q_swp = _es8.enter_context(nc.sbuf_tensor("p8_qs", [128, 1024], BF16))
            q_cos = _es8.enter_context(nc.sbuf_tensor("p8_qc", [128, 1024], BF16))
            q_rope = _es8.enter_context(nc.sbuf_tensor("p8_qq", [128, 1024], BF16))
            probs1 = _es8.enter_context(nc.sbuf_tensor("p8_pb", [128, 2, 4, 256], BF16))
            rc_t = _es8.enter_context(nc.sbuf_tensor("p8_rc", [128, 4], F32))
            cf_t = _es8.enter_context(nc.sbuf_tensor("p8_cf", [128, 4], F32))
            tmp_t = _es8.enter_context(nc.sbuf_tensor("p8_tmp", [128, 2, 128], F32))
            stage = _es8.enter_context(nc.sbuf_tensor("p8_stg", [128, 2, 2, 128], BF16))
            q0 = _es8.enter_context(nc.psum_tensor("p8_q0", [128, 512], F32))
            q1 = _es8.enter_context(nc.psum_tensor("p8_q1", [128, 512], F32))
            s0 = _es8.enter_context(nc.psum_tensor("p8_s0", [128, 512], F32))
            s1 = _es8.enter_context(nc.psum_tensor("p8_s1", [128, 512], F32))
            a0 = _es8.enter_context(nc.psum_tensor("p8_a0", [128, 512], F32))
            a1 = _es8.enter_context(nc.psum_tensor("p8_a1", [128, 512], F32))
            a2 = _es8.enter_context(nc.psum_tensor("p8_a2", [128, 512], F32))
            a3 = _es8.enter_context(nc.psum_tensor("p8_a3", [128, 512], F32))
            if True:
                qpool = PsumPool([q0, q1])
                spool = PsumPool([s0, s1])
                apool = PsumPool([a0, a1, a2, a3])
                w_guard = [None, None]
                probs_guard = [None, None]
                stage_guard = [None, None]
                t_sw_prev = None
                scores_prev = None
                for h in range(32):
                    wb = h % 2
                    if w_guard[wb] is not None:
                        b.wt("gpsimd", w_guard[wb][0], w_guard[wb][1])
                    t_w = b.dma("gpsimd", "sw", w1q_t[:, wb, :, :].rearrange("p c o -> p (c o)"),
                                w1q[h].rearrange("p c o -> p (c o)"))
                    b.wt("tensor", "sw", t_w)
                    t_qe = None
                    for qb in range(2):
                        ps = qpool.acq(b, qb)
                        for dc in range(32):
                            t_qm = b.ins("tensor", lambda e, o=ps[:], l=w1q_t[:, wb, dc, :],
                                         r=xn1[:, dc, qb * 512:(qb + 1) * 512],
                                         s=(dc == 0), st_=(dc == 31):
                                         e.matmul(o, l, r, start=s, stop=st_),
                                         "sm" if dc == 31 else None)
                        if qb == 1:
                            w_guard[wb] = ("sm", t_qm)
                        b.wt("vector", "sm", t_qm)
                        if qb == 0 and t_sw_prev is not None:
                            b.wt("vector", "st", t_sw_prev)
                        t_qe = b.ins("vector", lambda e, a=q_raw[:, qb * 512:(qb + 1) * 512], p=ps[:]:
                                     e.tensor_copy(a, p), "sv")
                        qpool.rel(qb, "sv", t_qe)
                    b.wt("sync", "sv", t_qe)
                    b.dma("sync", "st", q_swp[0:64, :], q_raw[64:128, :])
                    t_sw = b.dma("sync", "st", q_swp[64:128, :], q_raw[0:64, :])
                    t_sw_prev = t_sw
                    if scores_prev is not None:
                        b.wt("vector", "sm", scores_prev)
                    b.wt("vector", "st", t_sw)
                    b.ins("vector", lambda e, a=q_cos[:], i0=q_raw[:], i1=cosq_t[:]:
                          e.tensor_tensor(out=a, in0=i0, in1=i1, op=aop.mult))
                    t_r2 = b.ins("vector", lambda e, a=q_swp[:], i1=sinq_t[:]:
                                 e.tensor_tensor(out=a, in0=a, in1=i1, op=aop.mult), "sv")
                    b.wt("vector", "sv", t_r2)
                    t_rope = b.ins("vector", lambda e, a=q_rope[:], i0=q_cos[:], i1=q_swp[:]:
                                   e.tensor_tensor(out=a, in0=i0, in1=i1, op=aop.add), "sv")
                    b.wt("tensor", "sv", t_rope)

                    for bl in range(4):
                        t_sgm = {}
                        for win in range(2):
                            cb = 2 * (bl + win)
                            if probs_guard[win] is not None:
                                b.wt("scalar", probs_guard[win][0], probs_guard[win][1])
                            for half2 in range(2):
                                for kc2 in range(2):
                                    kc = 2 * half2 + kc2
                                    ps_s = spool.acq(b, kc % 2)
                                    t_sc = b.ins("tensor", lambda e, o=ps_s[:, 0:256],
                                                 l=Kr1[:, cb + kc, :],
                                                 r=q_rope[:, bl * 256:(bl + 1) * 256]:
                                                 e.matmul(o, l, r, start=True, stop=True),
                                                 "sm" if kc2 == 1 else None)
                                scores_prev = t_sc
                                b.wt("scalar", "sm", t_sc)
                                for kc2 in range(2):
                                    kc = 2 * half2 + kc2
                                    t_sig = b.ins("scalar", lambda e, a=probs1[:, win, kc, :],
                                                  p=spool.t[kc % 2][:, 0:256], s=SCL:
                                                  e.activation(out=a, in_=p, func=AF.Sigmoid, scale=s),
                                                  "sc" if kc2 == 1 else None)
                                spool.rel(0, "sc", t_sig)
                                spool.rel(1, "sc", t_sig)
                            t_sgm[win] = t_sig
                        t_av = None
                        for win in range(2):
                            cb = 2 * (bl + win)
                            b.wt("tensor", "sc", t_sgm[win])
                            for qt in range(2):
                                ai = win * 2 + qt
                                apool.acq(b, ai)
                                for kc in range(4):
                                    t_av = b.ins("tensor", lambda e, o=apool.t[ai][:, 0:129],
                                                 l=probs1[:, win, kc, qt * 128:(qt + 1) * 128],
                                                 r=V1g[:, cb + kc, 0:129],
                                                 s=(kc == 0), st_=(kc == 3):
                                                 e.matmul(o, l, r, start=s, stop=st_),
                                                 "sm" if kc == 3 else None)
                            probs_guard[win] = ("sm", t_av)
                        b.wt("vector", "sm", t_av)
                        sb_ = bl % 2
                        if stage_guard[sb_] is not None:
                            b.wt("vector", stage_guard[sb_][0], stage_guard[sb_][1])
                        for ai in range(4):
                            t_rc = b.ins("vector", lambda e, a=rc_t[:, ai:ai + 1],
                                         p=apool.t[ai][:, 128:129]: e.reciprocal(a, p),
                                         "sv" if ai == 3 else None)
                        b.wt("vector", "sv", t_rc)
                        for qt in range(2):
                            jq = bl * 2 + qt
                            for win in range(2):
                                ai = win * 2 + qt
                                coef = omal_t if win == 0 else alph_t
                                b.ins("vector", lambda e, a=cf_t[:, ai:ai + 1], r=rc_t[:, ai:ai + 1],
                                      c=coef[:, jq:jq + 1]:
                                      e.tensor_tensor(out=a, in0=r, in1=c, op=aop.mult))
                        for qt in range(2):
                            t_tp = b.ins("vector", lambda e, a=tmp_t[:, qt, :], p=apool.t[2 + qt][:, 0:128],
                                         c=cf_t[:, 2 + qt:3 + qt]:
                                         e.tensor_scalar(out=a, in0=p, scalar1=c, scalar2=None, op0=aop.mult),
                                         "sv" if qt == 1 else None)
                        b.wt("vector", "sv", t_tp)
                        t_st = None
                        for qt in range(2):
                            t_st = b.ins("vector", lambda e, a=stage[:, sb_, qt, :],
                                         p=apool.t[qt][:, 0:128], c=cf_t[:, qt:qt + 1],
                                         t_=tmp_t[:, qt, :]:
                                         e.scalar_tensor_tensor(out=a, in0=p, scalar=c, in1=t_,
                                                                op0=aop.mult, op1=aop.add), "sv")
                            apool.rel(qt, "sv", t_st)
                            apool.rel(2 + qt, "sv", t_st)
                        b.wt("scalar", "sv", t_st)
                        t_sp = b.dma("scalar", "sd",
                                     d_attn1[h][bl * 256:(bl + 1) * 256, :].rearrange("(t p) d -> p t d", p=128),
                                     stage[:, sb_, :, :])
                        stage_guard[sb_] = ("sd", t_sp)
                b.barrier()
            _es8.close()
        p78_stack.close()

        if stages >= 9:
            _es9 = contextlib.ExitStack()
            at_fm = _es9.enter_context(nc.sbuf_tensor("p9_att", [128, 32, 1024], BF16))
            wraw = _es9.enter_context(nc.sbuf_tensor("p9_raw", [128, 2, 32, 128], BF16))
            stage = _es9.enter_context(nc.sbuf_tensor("p9_stg", [128, 2, 512], BF16))
            o0 = _es9.enter_context(nc.psum_tensor("p9_p0", [128, 512], F32))
            o1 = _es9.enter_context(nc.psum_tensor("p9_p1", [128, 512], F32))
            if True:
                opool = PsumPool([o0, o1])
                for ac in range(32):
                    t_at = b.dmat(at_fm[:, ac, :], d_attn1[ac])
                b.wt("tensor", "st", t_at)
                w_guard = [None, None]
                stage_guard = [None, None]
                for ob in range(32):
                    wb = ob % 2
                    if w_guard[wb] is not None:
                        b.wt("gpsimd", w_guard[wb][0], w_guard[wb][1])
                    t_w = b.dma("gpsimd", "sw", wraw[:, wb, :, :].rearrange("p c o -> p (c o)"),
                                w1o[ob].rearrange("p c o -> p (c o)"))
                    b.wt("tensor", "sw", t_w)
                    for qb in range(2):
                        ps = opool.acq(b, qb)
                        for ac in range(32):
                            t_om = b.ins("tensor", lambda e, o=ps[:], l=wraw[:, wb, ac, :],
                                         r=at_fm[:, ac, qb * 512:(qb + 1) * 512],
                                         s=(ac == 0), st_=(ac == 31):
                                         e.matmul(o, l, r, start=s, stop=st_),
                                         "sm" if ac == 31 else None)
                        if qb == 1:
                            w_guard[wb] = ("sm", t_om)
                        b.wt("vector", "sm", t_om)
                        if stage_guard[qb] is not None:
                            b.wt("vector", stage_guard[qb][0], stage_guard[qb][1])
                        t_ev = b.ins("vector", lambda e, a=stage[:, qb, :], p=ps[:]:
                                     e.tensor_copy(a, p), "sv")
                        opool.rel(qb, "sv", t_ev)
                        b.wt("scalar", "sv", t_ev)
                        t_sp = b.dma("scalar", "sd",
                                     d_attnf1[ob * 128:(ob + 1) * 128, qb * 512:(qb + 1) * 512],
                                     stage[:, qb, :])
                        stage_guard[qb] = ("sd", t_sp)
                b.barrier()
            _es9.close()

        if stages >= 10:
            _es10 = contextlib.ExitStack()
            gat = _es10.enter_context(nc.sbuf_tensor("pa_gt", [128, 2, 2, 1024], BF16))
            up_t = _es10.enter_context(nc.sbuf_tensor("pa_up", [128, 2, 2, 1024], BF16))
            dn_t = _es10.enter_context(nc.sbuf_tensor("pa_dn", [128, 2, 8, 256], BF16))
            hact = _es10.enter_context(nc.sbuf_tensor("pa_h", [128, 2, 8, 1024], BF16))
            y_t = _es10.enter_context(nc.sbuf_tensor("pa_y", [128, 2, 2, 1024], BF16))
            u0 = _es10.enter_context(nc.psum_tensor("pa_u0", [128, 512], F32))
            u1 = _es10.enter_context(nc.psum_tensor("pa_u1", [128, 512], F32))
            d0 = _es10.enter_context(nc.psum_tensor("pa_d0", [128, 512], F32))
            d1 = _es10.enter_context(nc.psum_tensor("pa_d1", [128, 512], F32))
            if True:
                upool = PsumPool([u0, u1])
                dpool = PsumPool([d0, d1])
                gat_guard = [None, None]
                w_guard = [None, None]
                h_guard = [None, None]
                y_guard = [None, None]
                for hh in range(16):
                    wb = hh % 2
                    if gat_guard[wb] is not None:
                        b.wt("gpsimd", gat_guard[wb][0], gat_guard[wb][1])
                    if w_guard[wb] is not None:
                        b.wt("gpsimd", w_guard[wb][0], w_guard[wb][1])
                    t_g = None
                    for rc in range(2):
                        t_g = b.ins("gpsimd", lambda e, o=gat[:, wb, rc, :], i=d_A,
                                    ix=perm_t[:, 2 * hh + rc:2 * hh + rc + 1]:
                                    e.indirect_dma_start(out=o, out_offset=None, in_=i,
                                                         in_offset=IndirectOffsetOnAxis(ap=ix, axis=0)),
                                    "sw", 16)
                    t_uw = b.dma("gpsimd", "sw", up_t[:, wb, :, :].rearrange("p c o -> p (c o)"),
                                 up_w[hh].rearrange("p c o -> p (c o)"))
                    t_dw = b.dma("gpsimd", "sw", dn_t[:, wb, :, :].rearrange("p c o -> p (c o)"),
                                 dn_w[hh].rearrange("p c o -> p (c o)"))
                    b.wt("tensor", "sw", t_uw)
                    b.wt("tensor", "sw", t_g)
                    if h_guard[wb] is not None:
                        b.wt("scalar", h_guard[wb][0], h_guard[wb][1])
                    t_ge = None
                    t_um = None
                    for mc in range(8):
                        for tb in range(2):
                            ps = upool.acq(b, (mc * 2 + tb) % 2)
                            for rc in range(2):
                                t_um = b.ins("tensor", lambda e, o=ps[:],
                                             l=up_t[:, wb, rc, mc * 128:(mc + 1) * 128],
                                             r=gat[:, wb, rc, tb * 512:(tb + 1) * 512],
                                             s=(rc == 0), st_=(rc == 1):
                                             e.matmul(o, l, r, start=s, stop=st_),
                                             "sm" if rc == 1 else None)
                            b.wt("scalar", "sm", t_um)
                            t_ge = b.ins("scalar", lambda e, a=hact[:, wb, mc, tb * 512:(tb + 1) * 512],
                                         p=ps[:]: e.activation(out=a, in_=p, func=AF.Gelu), "sc")
                            upool.rel((mc * 2 + tb) % 2, "sc", t_ge)
                    gat_guard[wb] = ("sm", t_um)
                    b.wt("tensor", "sc", t_ge)
                    b.wt("tensor", "sw", t_dw)
                    if y_guard[wb] is not None:
                        b.wt("vector", y_guard[wb][0], y_guard[wb][1])
                    t_ye = None
                    t_dm = None
                    for rc in range(2):
                        for tb in range(2):
                            ps = dpool.acq(b, (rc * 2 + tb) % 2)
                            for mc in range(8):
                                t_dm = b.ins("tensor", lambda e, o=ps[:],
                                             l=dn_t[:, wb, mc, rc * 128:(rc + 1) * 128],
                                             r=hact[:, wb, mc, tb * 512:(tb + 1) * 512],
                                             s=(mc == 0), st_=(mc == 7):
                                             e.matmul(o, l, r, start=s, stop=st_),
                                             "sm" if mc == 7 else None)
                            b.wt("vector", "sm", t_dm)
                            t_ye = b.ins("vector", lambda e,
                                         a=y_t[:, wb, rc, tb * 512:(tb + 1) * 512], p=ps[:]:
                                         e.tensor_copy(a, p), "sv")
                            dpool.rel((rc * 2 + tb) % 2, "sv", t_ye)
                    h_guard[wb] = ("sm", t_dm)
                    w_guard[wb] = ("sm", t_dm)
                    b.wt("gpsimd", "sv", t_ye)
                    t_sc2 = None
                    for rc in range(2):
                        t_sc2 = b.ins("gpsimd", lambda e, o=d_ffn1, i=y_t[:, wb, rc, :],
                                      ix=perm_t[:, 2 * hh + rc:2 * hh + rc + 1]:
                                      e.indirect_dma_start(out=o,
                                                           out_offset=IndirectOffsetOnAxis(ap=ix, axis=0),
                                                           in_=i, in_offset=None),
                                      "sw", 16)
                    y_guard[wb] = ("sw", t_sc2)
                b.barrier()
            _es10.close()

        if stages >= 11:
            _es11 = contextlib.ExitStack()
            x1t = _es11.enter_context(nc.sbuf_tensor("pb_x1", [128, 2, D], F32))
            att = _es11.enter_context(nc.sbuf_tensor("pb_at", [128, 2, D], BF16))
            fft = _es11.enter_context(nc.sbuf_tensor("pb_ff", [128, 2, D], BF16))
            ot = _es11.enter_context(nc.sbuf_tensor("pb_o", [128, 2, D], F32))
            if True:
                prev = {}
                for tt in range(8):
                    bi = tt % 2
                    pg = prev.get(bi)
                    if pg is not None:
                        b.wt("sync", "so", pg)
                    b.dma("sync", "st", x1t[:, bi, :], d_x1[tt * 128:(tt + 1) * 128, :])
                    b.dmat(att[:, bi, :], d_attnf1[:, tt * 128:(tt + 1) * 128])
                    t_f = b.dmat(fft[:, bi, :], d_ffn1[:, tt * 128:(tt + 1) * 128])
                    b.wt("vector", "st", t_f)
                    if pg is not None:
                        b.wt("vector", "so", pg)
                    t_b1 = b.ins("vector", lambda e, a=ot[:, bi, :], i0=att[:, bi, :], i1=fft[:, bi, :]:
                                 e.tensor_tensor(out=a, in0=i0, in1=i1, op=aop.add), "sv")
                    b.wt("vector", "sv", t_b1)
                    t_o = b.ins("vector", lambda e, a=ot[:, bi, :], i1=x1t[:, bi, :]:
                                e.tensor_tensor(out=a, in0=a, in1=i1, op=aop.add), "sv")
                    b.wt("sync", "sv", t_o)
                    t_out = b.dma("sync", "so", out_t[tt * 128:(tt + 1) * 128, :], ot[:, bi, :])
                    prev[bi] = t_out
            _es11.close()
            b.wt("sync", "so", b.cnt.get("so", 0))

        with nc.Block() as block:
            b.replay(block)
    return nc



def _img(wt, ob):
    d, o = wt.shape
    return np.ascontiguousarray(
        np.asarray(wt, np.float32).reshape(d // 128, 128, o // ob, ob).transpose(2, 1, 0, 3))


def _pp(v):
    v = np.asarray(v, np.float32)
    n = v.shape[0] // 128
    return np.ascontiguousarray(v.reshape(n, 128).T)


def _rope_cache():
    Sq, Dh = 2048, 128
    inv_freq = 1.0 / 10000.0 ** (np.arange(0, Dh, 2, dtype=np.float32) / Dh)
    t = np.arange(Sq, dtype=np.float32)
    f = np.outer(t, inv_freq)
    cos = np.zeros((Sq, Dh), dtype=np.float32)
    sin = np.zeros((Sq, Dh), dtype=np.float32)
    cos[:, 0::2] = np.cos(f); cos[:, 1::2] = np.cos(f)
    sin[:, 0::2] = np.sin(f); sin[:, 1::2] = np.sin(f)
    return cos, sin


def make_inputs(ins):
    f32 = np.float32
    A = lambda k: np.asarray(ins[k], f32)
    shared = {}
    shared["wq"] = _img(A("L0_attn_q_weight").T, 128)
    shared["wk"] = _img(A("L0_attn_k_weight").T, 512)
    shared["wv"] = _img(A("L0_attn_v_weight").T, 512)
    shared["wo"] = _img(A("L0_attn_o_weight").T, 128)
    shared["wf"] = _img(A("L0_ffn_W").T, 128)
    la = A("L0_ffn_lora_A").reshape(128, 4096)
    shared["la"] = _img(la.T, 128)
    lbT = A("L0_ffn_lora_B").transpose(0, 2, 1).reshape(128, 4096)
    shared["lb"] = np.ascontiguousarray(lbT.reshape(128, 32, 128))
    shared["w1q"] = _img(A("L1_attn_Wq").T, 128)
    shared["w1k"] = _img(A("L1_attn_Wk").T, 128)
    shared["w1v"] = _img(A("L1_attn_Wv").T, 128)
    shared["w1o"] = _img(A("L1_attn_Wo").T, 128)
    shared["up"] = np.ascontiguousarray(
        A("L1_ffn_up").reshape(16, 2, 128, 1024).transpose(0, 2, 1, 3))
    shared["dn"] = np.ascontiguousarray(
        A("L1_ffn_down").reshape(16, 8, 128, 256).transpose(0, 2, 1, 3))
    shared["qs"] = _pp(ins["L0_attn_q_scale"])
    shared["ks"] = _pp(ins["L0_attn_k_scale"])
    shared["vs"] = _pp(ins["L0_attn_v_scale"])
    shared["os"] = _pp(ins["L0_attn_o_scale"])
    shared["fb"] = _pp(ins["L0_ffn_bias"])
    shared["lw"] = np.ascontiguousarray(
        np.repeat(np.asarray(ins["L0_ffn_lora_weights"], f32), 16)[:, None])
    shared["onesb"] = np.ones((128, 1), f32)
    consts = np.zeros((128, 8), f32)
    consts[:, 0] = 1e-5
    consts[:, 1] = RSCALE
    consts[:, 7] = 1.0
    shared["consts"] = consts
    shared["perm"] = np.ascontiguousarray(
        np.asarray(ins["L1_ffn_perm"]).astype(np.int64).astype(np.uint32))

    cosc, sinc = _rope_cache()
    lin = np.linspace(0.0, 1.0, 256, dtype=f32)
    x = np.asarray(ins["L0_attn_hidden_states"], f32)
    sgn = np.concatenate([-np.ones(64, f32), np.ones(64, f32)])

    maps = []
    for c in range(8):
        bidx, h = c // 2, c % 2
        m = dict(shared)
        own = np.arange(1024 * h, 1024 * h + 1024)
        halo = np.arange(1024, 1280) if h == 0 else np.arange(768, 1024)
        ext = np.concatenate([own, halo])
        oth = np.arange(1280, 2048) if h == 0 else np.arange(0, 768)
        xb = x[bidx]
        m["x_ext"] = np.ascontiguousarray(xb[ext])
        m["x_oth"] = np.ascontiguousarray(xb[oth])
        m["cosq"] = np.ascontiguousarray(cosc[own].T)
        m["sinq"] = np.ascontiguousarray(sinc[own].T * sgn[:, None])
        m["cosk"] = np.ascontiguousarray(cosc[ext].T)
        m["sink"] = np.ascontiguousarray(sinc[ext].T * sgn[:, None])
        alph = np.zeros((128, 8), f32)
        omal = np.zeros((128, 8), f32)
        for j in range(8):
            a = lin[(j % 2) * 128:(j % 2) * 128 + 128]
            alph[:, j] = a
            omal[:, j] = 1.0 - a
        if h == 0:
            alph[:, 0] = 1.0; alph[:, 1] = 1.0
            omal[:, 0] = 0.0; omal[:, 1] = 0.0
        m["alph"] = alph
        m["omal"] = omal
        vaug = np.zeros((128, 12), f32)
        vaug[:, 2:10] = 1.0
        if h == 0:
            vaug[:, 10:12] = 1.0
        else:
            vaug[:, 0:2] = 1.0
        m["vaug"] = vaug
        maps.append(m)
    return maps


def kernel(**inputs):
    maps = make_inputs(inputs)
    nc = build_core(dbg=False)
    res = run_bass_kernel_spmd(nc, maps, core_ids=list(range(8)))
    out = np.zeros((4, 2048, 4096), np.float32)
    for c in range(8):
        bidx, h = c // 2, c % 2
        out[bidx, 1024 * h:1024 * (h + 1), :] = res.results[c]["out"]
    return out
